# version 02
# baseline (speedup 1.0000x reference)
import contextlib
import numpy as np
import ml_dtypes

import concourse.bass as bass
import concourse.mybir as mybir
from concourse.bass_utils import run_bass_kernel_spmd
from concourse.alu_op_type import AluOpType as ALU
from concourse.aot_env import allow_ici_all_to_all

BF16 = ml_dtypes.bfloat16
F32 = mybir.dt.float32
BF = mybir.dt.bfloat16
AF = mybir.ActivationFunctionType

D = 2048
KC = 16
CH = 128
RT2 = float(128.0 ** -0.5)

ENGS = ("sync", "vector", "scalar", "tensor", "gpsimd")
STAGES = ("attn_a", "ffn1", "attn_b", "ffn2")


class Prog:
    def __init__(self):
        self.ops = {e: [] for e in ENGS}
        self.semv = {}

    def val(self, name):
        return self.semv.setdefault(name, 0)

    def op(self, eng, fn, waits=(), inc=None, inc_amt=1, tag="", kind="c"):
        if inc is not None:
            self.semv[inc] = self.semv.setdefault(inc, 0) + inc_amt
        ws = []
        for s, v in waits:
            self.semv.setdefault(s, 0)
            if v > 0:
                ws.append((s, v))
        self.ops[eng].append((ws, fn, inc, inc_amt, tag, kind))
        return self.semv[inc] if inc is not None else None

    def dma(self, eng, out, in_, waits=(), inc=None, tag=""):
        return self.op(eng, lambda e, o=out, i=in_: e.dma_start(out=o, in_=i),
                       waits=waits, inc=inc, inc_amt=16, tag=tag, kind="d")

    def barrier(self):
        snap = [(s, v) for s, v in self.semv.items() if v > 0]
        for e in ENGS:
            self.ops[e].append(("BARRIER", snap))

    def emit(self, nc, block, stk):
        sems = {name: stk.enter_context(nc.semaphore(name))
                for name in self.semv}

        selfwait = {"vector", "scalar", "gpsimd"}
        cur = {}

        def mk(eng_name):
            def body(eng):
                pending = []
                prevc = None
                for rec in self.ops[eng_name]:
                    if rec[0] == "BARRIER":
                        pending.extend(rec[1])
                        continue
                    ws, fn, inc, amt, tag, kind = rec
                    for s, v in pending:
                        eng.wait_ge(sems[s], v)
                    pending = []
                    if (kind == "c" and eng_name in selfwait
                            and prevc is not None):
                        eng.wait_ge(sems[prevc[0]], prevc[1])
                    for s, v in ws:
                        eng.wait_ge(sems[s], v)
                    ins = fn(eng)
                    if inc is not None:
                        ins.then_inc(sems[inc], amt)
                        cur[inc] = cur.get(inc, 0) + amt
                        if kind == "c":
                            prevc = (inc, cur[inc])
                for s, v in pending:
                    eng.wait_ge(sems[s], v)
            return body

        for e in ENGS:
            getattr(block, e)(mk(e))


def build(T_ALL=4096, stages=None):
    stages = STAGES if stages is None else stages
    TOK = T_ALL // 4
    NCHUNK = T_ALL // CH
    TH = TOK // 512
    assert TOK % 512 == 0 and T_ALL % CH == 0

    nc = bass.Bass("TRN2", target_bir_lowering=False)
    p = Prog()

    rel = {}

    def rd(name, sem, v):
        rel.setdefault(name, []).append((sem, v))

    def wr(name):
        return rel.pop(name, [])

    def din(name, shape, dt=F32):
        return nc.dram_tensor(name, shape, dt, kind="ExternalInput").ap()

    def dint(name, shape, dt=F32):
        return nc.dram_tensor(name, shape, dt).ap()

    x_in = din("x", [128, KC, TOK])
    y_out = nc.dram_tensor("y", [128, KC, TOK], F32, kind="ExternalOutput").ap()

    wqkvg_a = din("wqkvg_a", [64, KC, 128, 128])
    wab_a = din("wab_a", [1, KC, 128, 32])
    wo_a = din("wo_a", [KC, KC, 128, 128])
    wqkvg_b = din("wqkvg_b", [40, KC, 128, 128])
    wab_b = din("wab_b", [1, KC, 128, 32])
    wo_b = din("wo_b", [KC, KC, 128, 128])
    wgu = din("wgu", [128, KC, 128, 128])
    wdn = din("wdn", [KC, 64, 128, 128])

    cva = din("cva", [128, 3, 2, 4])
    cba = din("cba", [128, 3, 2])
    cvb = din("cvb", [128, 4, 4])
    abb_a = din("abb_a", [32, 1])
    lnwb_a = din("lnwb_a", [128, 2])
    onw_b = din("onw_b", [128, 1])
    identb = din("identb", [128, 128], BF)
    identf = din("identf", [128, 128])
    ones_b = din("ones_b", [128, 1], BF)
    maskgt = din("maskgt", [128, 128])
    maskgu = din("maskgu", [128, 128])
    maskdt = din("maskdt", [128, 128])

    ex_src = {"attn_a": dint("exa_s", [8, 6, 128, TOK], BF),
              "attn_b": dint("exb_s", [8, 4, 128, TOK], BF)}
    ex_dst = {"attn_a": dint("exa_d", [8, 6, 128, TOK], BF),
              "attn_b": dint("exb_d", [8, 4, 128, TOK], BF)}
    rw_src = {"attn_a": dint("rwa_s", [8, 4, TOK]),
              "attn_b": dint("rwb_s", [8, 4, TOK])}
    rw_dst = {"attn_a": dint("rwa_d", [8, 4, TOK]),
              "attn_b": dint("rwb_d", [8, 4, TOK])}
    ox_src = {"attn_a": dint("oxa_s", [8, 2, 128, TOK], BF),
              "attn_b": dint("oxb_s", [8, 2, 128, TOK], BF)}
    ox_dst = {"attn_a": dint("oxa_d", [8, 2, 128, TOK], BF),
              "attn_b": dint("oxb_d", [8, 2, 128, TOK], BF)}
    flat = {"attn_a": dint("flata", [10, 4, T_ALL]),
            "attn_b": dint("flatb", [10, 4, T_ALL])}
    lnflat = dint("lnflat", [2, TOK])
    onflat = dint("onflat", [2, T_ALL])
    wgu_bf = dint("wgu_bfc", [128, KC, 128, 128], BF)
    wdn_bf = dint("wdn_bfc", [KC, 64, 128, 128], BF)

    gstk = contextlib.ExitStack()
    G = gstk.enter_context

    x_fm = G(nc.sbuf_tensor("x_fm", [128, KC, TOK], F32))
    s_identb = G(nc.sbuf_tensor("s_identb", [128, 128], BF))
    s_identf = G(nc.sbuf_tensor("s_identf", [128, 128], F32))
    s_onesb = G(nc.sbuf_tensor("s_onesb", [128, 1], BF))
    s_maskgt = G(nc.sbuf_tensor("s_maskgt", [128, 128], F32))
    s_maskgu = G(nc.sbuf_tensor("s_maskgu", [128, 128], F32))
    s_maskdt = G(nc.sbuf_tensor("s_maskdt", [128, 128], F32))
    s_cva = G(nc.sbuf_tensor("s_cva", [128, 3, 2, 4], F32))
    s_cba = G(nc.sbuf_tensor("s_cba", [128, 3, 2], F32))
    s_cvb = G(nc.sbuf_tensor("s_cvb", [128, 4, 4], F32))
    s_abba = G(nc.sbuf_tensor("s_abba", [32, 1], F32))
    s_lnwb = G(nc.sbuf_tensor("s_lnwb", [128, 2], F32))
    s_onwb = G(nc.sbuf_tensor("s_onwb", [128, 1], F32))
    s_cc = G(nc.sbuf_tensor("s_cc", [128, 4], F32))
    PS = [G(nc.psum_tensor(f"PS{i}", [128, 512], F32)) for i in range(7)]
    PSB = G(nc.psum_tensor("PSB", [128, 1024], BF))

    for t, a in [(s_identb, identb), (s_identf, identf), (s_onesb, ones_b),
                 (s_maskgt, maskgt), (s_maskgu, maskgu), (s_maskdt, maskdt),
                 (s_cva, cva), (s_cba, cba), (s_cvb, cvb), (s_abba, abb_a),
                 (s_lnwb, lnwb_a), (s_onwb, onw_b)]:
        p.dma("sync", t[:], a, inc="QS")
    p.dma("sync", x_fm[:], x_in, inc="QS")
    for j, v in enumerate([1e-5, 1.0, -0.5 * float(np.log(128.0)), 1e-6]):
        p.op("vector", lambda e, j=j, v=v: e.memset(s_cc[:, j:j + 1], v),
             inc="SV")
    p.barrier()

    def layernorm(tag, PE, xln):
        sqs = PE(nc.sbuf_tensor(f"{tag}sq", [128, KC, TOK], BF))
        murow = PE(nc.sbuf_tensor(f"{tag}mur", [1, TOK], F32))
        rsrow = PE(nc.sbuf_tensor(f"{tag}rsr", [1, TOK], F32))
        tmprow = PE(nc.sbuf_tensor(f"{tag}tmr", [1, TOK], F32))
        mu_b = PE(nc.sbuf_tensor(f"{tag}mub", [128, TOK], F32))
        rs_b = PE(nc.sbuf_tensor(f"{tag}rsb", [128, TOK], F32))
        lps = [PS[i][0:1, :] for i in range(2 * TH)]
        vc = p.op("vector", lambda e: e.tensor_copy(xln[:], x_fm[:]), inc="SV")
        sc = p.op("scalar", lambda e: e.activation(sqs[:], x_fm[:], AF.Square),
                  inc="SS")
        mm = []
        for st_i, (src, sem, v) in enumerate([(xln, "SV", vc), (sqs, "SS", sc)]):
            for th in range(TH):
                r = None
                for k in range(KC):
                    r = p.op("tensor",
                             lambda e, s=src, k=k, i=st_i * TH + th, th=th:
                             e.matmul(lps[i], s_onesb[:],
                                      s[:, k, th * 512:(th + 1) * 512],
                                      start=(k == 0), stop=(k == KC - 1)),
                             waits=[(sem, v)] if k == 0 else (),
                             inc="ST" if k == KC - 1 else None)
                mm.append(r)
        for th in range(TH):
            sl = slice(th * 512, (th + 1) * 512)
            p.op("vector", lambda e, th=th, sl=sl: e.tensor_scalar(
                murow[0:1, sl], lps[th], 1.0 / D, None, ALU.mult),
                waits=[("ST", mm[th])], inc="SV")
            p.op("vector", lambda e, th=th, sl=sl: e.tensor_scalar(
                rsrow[0:1, sl], lps[TH + th], 1.0 / D, None, ALU.mult),
                waits=[("ST", mm[TH + th])], inc="SV")
        p.op("vector", lambda e: e.tensor_tensor(
            tmprow[0:1, :], murow[0:1, :], murow[0:1, :], ALU.mult), inc="SV")
        vv = p.op("vector", lambda e: e.tensor_tensor(
            rsrow[0:1, :], rsrow[0:1, :], tmprow[0:1, :], ALU.subtract),
            inc="SV")
        s1 = p.op("scalar", lambda e: e.activation(
            rsrow[0:1, :], rsrow[0:1, :], AF.Ln, bias=s_cc[0:1, 0:1]),
            waits=[("SV", vv)], inc="SS")
        s2 = p.op("scalar", lambda e: e.activation(
            rsrow[0:1, :], rsrow[0:1, :], AF.Exp, scale=-0.5),
            waits=[("SS", s1)], inc="SS")
        p.dma("scalar", lnflat[0:1, :], murow[0:1, :],
              waits=[("SV", p.val("SV"))], inc="LQ")
        d2 = p.dma("scalar", lnflat[1:2, :], rsrow[0:1, :],
                   waits=[("SS", s2)], inc="LQ")
        b1 = p.dma("scalar", mu_b[:], lnflat[0:1, :].broadcast_to([128, TOK]),
                   waits=[("LQ", d2)], inc="LQ")
        b2 = p.dma("scalar", rs_b[:], lnflat[1:2, :].broadcast_to([128, TOK]),
                   inc="LQ")
        allmm = max(mm)
        last = None
        for k in range(KC):
            w1 = [("LQ", b2), ("ST", allmm), ("SS", sc)] if k == 0 else []
            gp = p.op("gpsimd", lambda e, k=k: e.tensor_tensor(
                sqs[:, k, :], x_fm[:, k, :], mu_b[:], ALU.subtract),
                waits=w1, inc="SG")
            last = p.op("vector", lambda e, k=k: e.tensor_tensor(
                xln[:, k, :], sqs[:, k, :], rs_b[:], ALU.mult),
                waits=[("SG", gp)], inc="SV")
        return last

    def wstream(tag, wt_ap, n_chunks, rhs, epi, wld, wbf, pp, cache_out=None,
                bf_src=False):
        convs, mms, evacs = [], {}, {}
        for n in range(n_chunks):
            b = n % 2
            if bf_src:
                w = [(f"{tag}M", mms[(n - 2, TH - 1)])] if n >= 2 else []
                ld = p.dma("sync", wbf[b][:].rearrange("p (k f) -> p k f", k=KC),
                           wt_ap[n].rearrange("k p f -> p k f"),
                           waits=w, inc=f"{tag}L{b}")
                cw = [(f"{tag}L{b}", ld)]
                convs.append(0)
            else:
                w = [(f"{tag}C", convs[n - 2])] if n >= 2 else []
                ld = p.dma("sync", wld[b][:].rearrange("p (k f) -> p k f", k=KC),
                           wt_ap[n].rearrange("k p f -> p k f"),
                           waits=w, inc=f"{tag}L{b}")
                w2 = [(f"{tag}L{b}", ld)]
                if n >= 2:
                    w2.append((f"{tag}M", mms[(n - 2, TH - 1)]))
                cv = p.op("vector", lambda e, b=b: e.tensor_copy(
                    wbf[b][:], wld[b][:]), waits=w2, inc=f"{tag}C")
                convs.append(cv)
                cw = [(f"{tag}C", cv)]
            if cache_out is not None:
                p.dma("scalar", cache_out[n].rearrange("k p f -> p k f"),
                      wbf[b][:].rearrange("p (k f) -> p k f", k=KC),
                      waits=cw, inc="WCQ")
            for th in range(TH):
                pidx = b * TH + th
                w3 = list(cw)
                if n >= 2:
                    w3.append((f"{tag}E", evacs[(n - 2, th)]))
                r = None
                for k in range(KC):
                    r = p.op("tensor", lambda e, b=b, k=k, th=th, pidx=pidx:
                             e.matmul(pp[pidx][:],
                                      wbf[b][:, k * 128:(k + 1) * 128],
                                      rhs[:, k, th * 512:(th + 1) * 512],
                                      start=(k == 0), stop=(k == KC - 1)),
                             waits=w3 if k == 0 else (),
                             inc=f"{tag}M" if k == KC - 1 else None)
                mms[(n, th)] = r
                evacs[(n, th)] = epi(n, th, pidx, [(f"{tag}M", r)])
        return evacs

    def attention(att):
        isa = att == "attn_a"
        tag = "A" if isa else "B"
        exs, exd = ex_src[att], ex_dst[att]
        rws, rwd = rw_src[att], rw_dst[att]
        oxs, oxd = ox_src[att], ox_dst[att]
        fl = flat[att]
        nproj = 64 if isa else 40

        with contextlib.ExitStack() as aph:
            APE = aph.enter_context
            g_fm = APE(nc.sbuf_tensor(f"{tag}gfm", [128, KC, TOK], BF))

            with contextlib.ExitStack() as ph:
                PE = ph.enter_context
                xln = PE(nc.sbuf_tensor(f"{tag}xln", [128, KC, TOK], BF))
                with contextlib.ExitStack() as lh:
                    layernorm(tag + "l", lh.enter_context, xln)
                    p.barrier()
                wld = [PE(nc.sbuf_tensor(f"{tag}wld{i}", [128, KC * 128], F32))
                       for i in range(2)]
                wbf = [PE(nc.sbuf_tensor(f"{tag}wbf{i}", [128, KC * 128], BF))
                       for i in range(2)]
                stg = [PE(nc.sbuf_tensor(f"{tag}stg{i}", [128, TOK], BF))
                       for i in range(4)]
                ab1 = PE(nc.sbuf_tensor(f"{tag}ab1", [32, TOK], F32))
                ab2 = PE(nc.sbuf_tensor(f"{tag}ab2", [32, TOK], F32))
                ab3 = PE(nc.sbuf_tensor(f"{tag}ab3", [32, TOK], F32))
                pp = PS
                pab = PS[4][0:32, :]
                stg_dma = {}

                def epi(n, th, pidx, w):
                    if isa:
                        ti, h = divmod(n, 16)
                    else:
                        if n < 16:
                            ti, h = 0, n
                        elif n < 20:
                            ti, h = 1, n - 16
                        elif n < 24:
                            ti, h = 2, n - 20
                        else:
                            ti, h = 3, n - 24
                    if ti == 3:
                        fn = AF.Sigmoid if isa else AF.Silu
                        return p.op("scalar",
                                    lambda e, h=h, th=th, pidx=pidx, fn=fn:
                                    e.activation(
                                        g_fm[:, h, th * 512:(th + 1) * 512],
                                        pp[pidx][:], fn),
                                    waits=w, inc=f"{tag}PE")
                    sb = n % 4
                    w2 = list(w)
                    if th == 0 and (n - 4) in stg_dma:
                        w2.append((f"{tag}SD{sb}", stg_dma[n - 4]))
                    ev = p.op("vector", lambda e, sb=sb, th=th, pidx=pidx:
                              e.tensor_copy(stg[sb][:, th * 512:(th + 1) * 512],
                                            pp[pidx][:]),
                              waits=w2, inc=f"{tag}PE")
                    if th == TH - 1:
                        if isa:
                            dsts = [(h // 2, ti * 2 + (h % 2))]
                        elif ti == 0:
                            dsts = [(h // 2, h % 2)]
                        else:
                            dsts = [(2 * h, ti + 1), (2 * h + 1, ti + 1)]
                        d = None
                        for dest, slot in dsts:
                            d = p.dma("scalar", exs[dest, slot], stg[sb][:],
                                      waits=[(f"{tag}PE", ev)],
                                      inc=f"{tag}SD{sb}")
                        stg_dma[n] = d
                    return ev

                wstream(tag + "P", wqkvg_a if isa else wqkvg_b, nproj, xln,
                        epi, wld, wbf, pp)

                abw = wab_a if isa else wab_b
                ldab = p.dma("sync",
                             wld[0][:, :KC * 32].rearrange("p (k f) -> p k f",
                                                           k=KC),
                             abw[0].rearrange("k p f -> p k f"),
                             waits=[(f"{tag}PC", p.val(f"{tag}PC"))], inc="QS")
                cvab = p.op("vector", lambda e: e.tensor_copy(
                    wbf[0][:, :KC * 32], wld[0][:, :KC * 32]),
                    waits=[("QS", ldab), (f"{tag}PM", p.val(f"{tag}PM"))],
                    inc="SV")
                prev_ev = None
                for th in range(TH):
                    r = None
                    for k in range(KC):
                        w0 = []
                        if k == 0:
                            w0 = [("SV", cvab),
                                  (f"{tag}PE", p.val(f"{tag}PE"))]
                            if prev_ev is not None:
                                w0.append(("SV", prev_ev))
                        r = p.op("tensor", lambda e, k=k, th=th: e.matmul(
                            pab, wbf[0][:, k * 32:(k + 1) * 32],
                            xln[:, k, th * 512:(th + 1) * 512],
                            start=(k == 0), stop=(k == KC - 1)),
                            waits=w0, inc="ST" if k == KC - 1 else None)
                    if isa:
                        prev_ev = p.op("vector", lambda e, th=th:
                                       e.tensor_scalar(
                                           ab1[:, th * 512:(th + 1) * 512],
                                           pab, s_abba[:], None, ALU.add),
                                       waits=[("ST", r)], inc="SV")
                    else:
                        prev_ev = p.op("vector", lambda e, th=th:
                                       e.tensor_copy(
                                           ab1[:, th * 512:(th + 1) * 512],
                                           pab),
                                       waits=[("ST", r)], inc="SV")
                s1 = p.op("scalar", lambda e: e.activation(
                    ab2[:], ab1[:], AF.Abs), waits=[("SV", prev_ev)], inc="SS")
                s2 = p.op("scalar", lambda e: e.activation(
                    ab2[:], ab2[:], AF.Exp, scale=-1.0),
                    waits=[("SS", s1)], inc="SS")
                s3 = p.op("scalar", lambda e: e.activation(
                    ab2[:], ab2[:], AF.Ln, bias=s_cc[0:32, 1:2]),
                    waits=[("SS", s2)], inc="SS")
                v1 = p.op("vector", lambda e: e.tensor_scalar(
                    ab3[:], ab1[:], 0.0, None, ALU.min),
                    waits=[("SV", prev_ev)], inc="SV")
                v2 = p.op("vector", lambda e: e.tensor_tensor(
                    ab3[:], ab3[:], ab2[:], ALU.subtract),
                    waits=[("SV", v1), ("SS", s3)], inc="SV")
                if not isa:
                    v2 = p.op("vector", lambda e: e.tensor_scalar(
                        ab3[0:16, :], ab3[0:16, :], 1.0 / 16.0, None,
                        ALU.mult), waits=[("SV", v2)], inc="SV")
                for h in range(16):
                    p.dma("scalar", rws[h // 2, h % 2], ab3[h:h + 1, :],
                          waits=[("SV", v2)] if h == 0 else (), inc="ABQ")
                    p.dma("scalar", rws[h // 2, 2 + h % 2],
                          ab3[16 + h:17 + h, :], inc="ABQ")

                allstg = [(f"{tag}SD{i}", p.val(f"{tag}SD{i}"))
                          for i in range(4)]
                with allow_ici_all_to_all():
                    p.op("gpsimd", lambda e: e.collective_compute(
                        "AllToAll", ALU.bypass,
                        replica_groups=[list(range(8))],
                        ins=[exs.rearrange("r a p t -> (r a p) t")],
                        outs=[exd.rearrange("r a p t -> (r a p) t")]),
                        waits=allstg, inc="CC")
                    p.op("gpsimd", lambda e: e.collective_compute(
                        "AllToAll", ALU.bypass,
                        replica_groups=[list(range(8))],
                        ins=[rws.rearrange("r a t -> (r a) t")],
                        outs=[rwd.rearrange("r a t -> (r a) t")]),
                        waits=[("ABQ", p.val("ABQ"))], inc="CC")
                p.barrier()

            scan(isa, tag, exd, rwd, oxs, fl)

            with allow_ici_all_to_all():
                cc3 = p.op("gpsimd", lambda e: e.collective_compute(
                    "AllToAll", ALU.bypass, replica_groups=[list(range(8))],
                    ins=[oxs.rearrange("r a p t -> (r a p) t")],
                    outs=[oxd.rearrange("r a p t -> (r a p) t")]),
                    waits=[("OXQ", p.val("OXQ"))], inc="CC")
            with contextlib.ExitStack() as ph:
                PE = ph.enter_context
                o_fm = PE(nc.sbuf_tensor(f"{tag}ofm", [128, KC, TOK], BF))
                owld = [PE(nc.sbuf_tensor(f"{tag}owl{i}", [128, KC * 128],
                                          F32)) for i in range(2)]
                owbf = [PE(nc.sbuf_tensor(f"{tag}owb{i}", [128, KC * 128],
                                          BF)) for i in range(2)]
                for h in range(16):
                    p.dma("scalar", o_fm[:, h, :], oxd[h // 2, h % 2],
                          waits=[("CC", cc3)] if h == 0 else (), inc="OAQ")
                og = p.op("vector", lambda e: e.tensor_tensor(
                    o_fm[:], o_fm[:], g_fm[:], ALU.mult),
                    waits=[("OAQ", p.val("OAQ"))], inc="SV")
                p.barrier()

                def epo(n, th, pidx, w):
                    return p.op("vector", lambda e, n=n, th=th, pidx=pidx:
                                e.tensor_tensor(
                                    x_fm[:, n, th * 512:(th + 1) * 512],
                                    PS[pidx][:],
                                    x_fm[:, n, th * 512:(th + 1) * 512],
                                    ALU.add),
                                waits=w, inc=f"{tag}OE")

                wstream(tag + "O", wo_a if isa else wo_b, KC, o_fm, epo,
                        owld, owbf, PS)
                p.barrier()

    def scan(isa, tag, exd, rwd, oxs, fl):
        with contextlib.ExitStack() as ph:
            PE = ph.enter_context
            with contextlib.ExitStack() as rh:
                RE = rh.enter_context
                rA = RE(nc.sbuf_tensor(f"{tag}rA", [4, T_ALL], F32))
                rB = RE(nc.sbuf_tensor(f"{tag}rB", [4, T_ALL], F32))
                rC = RE(nc.sbuf_tensor(f"{tag}rC", [4, T_ALL], F32))
                rD = RE(nc.sbuf_tensor(f"{tag}rD", [4, T_ALL], F32))
                zrow = RE(nc.sbuf_tensor(f"{tag}zrw", [4, CH], F32))
                mz = p.op("vector", lambda e: e.memset(zrow[:], 0.0), inc="SV")
                da = None
                for c in range(4):
                    b, hh = c // 2, c % 2
                    for qq in range(4):
                        da = p.dma("scalar",
                                   rA[c:c + 1, qq * TOK:(qq + 1) * TOK],
                                   rwd[4 * b + qq, hh],
                                   waits=[("CC", p.val("CC"))]
                                   if da is None else (), inc="FLQ")
                        da = p.dma("scalar",
                                   rB[c:c + 1, qq * TOK:(qq + 1) * TOK],
                                   rwd[4 * b + qq, 2 + hh], inc="FLQ")
                def fdma(dst, src, waits=()):
                    return p.dma("scalar", dst, src,
                                 waits=list(waits) + [("FLQ", p.val("FLQ"))],
                                 inc="FLQ")

                vL = None
                for ci in range(NCHUNK):
                    sl = slice(ci * CH, (ci + 1) * CH)
                    vL = p.op("vector", lambda e, sl=sl: e.tensor_tensor_scan(
                        rC[:, sl], rA[:, sl], zrow[:], 0.0, ALU.add,
                        ALU.bypass),
                        waits=[("FLQ", da), ("SV", mz)] if ci == 0 else (),
                        inc="SV")
                dL = fdma(fl[6], rC[:], [("SV", vL)])
                sAL = p.op("scalar", lambda e: e.activation(
                    rD[:], rC[:], AF.Exp), waits=[("SV", vL)], inc="SS")
                dAL = fdma(fl[7], rD[:], [("SS", sAL)])
                vnL = p.op("vector", lambda e: e.tensor_scalar(
                    rD[:], rC[:], -1.0, None, ALU.mult),
                    waits=[("FLQ", dAL)], inc="SV")
                dnL = fdma(fl[0], rD[:], [("SV", vnL)])
                if isa:
                    p.op("vector", lambda e: e.tensor_tensor(
                        rA[:], rC[:], rA[:], ALU.subtract),
                        waits=[("SV", vL)], inc="SV")
                    vLb = p.op("vector", lambda e: e.tensor_tensor(
                        rA[:], rA[:], rB[:], ALU.add), inc="SV")
                else:
                    vLb = p.op("vector", lambda e: e.tensor_tensor(
                        rA[:], rC[:], rB[:], ALU.add),
                        waits=[("SV", vL)], inc="SV")
                dLb = fdma(fl[1], rA[:], [("SV", vLb)])
                sBt = p.op("scalar", lambda e: e.activation(
                    rB[:], rB[:], AF.Exp), waits=[("SV", vLb)], inc="SS")
                dBt = fdma(fl[3], rB[:], [("SS", sBt)])
                sc_ = p.op("scalar", lambda e: e.activation(
                    rA[:], rA[:], AF.Exp), waits=[("FLQ", dLb)], inc="SS")
                dc = fdma(fl[2], rA[:], [("SS", sc_)])
                vAC = None
                for ci in range(NCHUNK):
                    sl = slice(ci * CH, (ci + 1) * CH)
                    e_ = ci * CH + CH - 1
                    vAC = p.op("vector", lambda e, sl=sl, e_=e_: e.tensor_copy(
                        rB[:, sl], rC[:, e_:e_ + 1].broadcast_to([4, CH])),
                        waits=[("FLQ", dBt)] if ci == 0 else (), inc="SV")
                vAb = p.op("vector", lambda e: e.tensor_tensor(
                    rA[:], rB[:], rC[:], ALU.subtract),
                    waits=[("SV", vAC), ("FLQ", dc)], inc="SV")
                sAb = p.op("scalar", lambda e: e.activation(
                    rA[:], rA[:], AF.Exp), waits=[("SV", vAb)], inc="SS")
                fdma(fl[4], rA[:], [("SS", sAb)])
                sAC = p.op("scalar", lambda e: e.activation(
                    rB[:], rB[:], AF.Exp), waits=[("SV", vAb)], inc="SS")
                fdma(fl[5], rB[:], [("SS", sAC)])
                p.barrier()

            zp = PE(nc.sbuf_tensor(f"{tag}zp", [128, T_ALL + 4], BF))
            acc0 = PE(nc.sbuf_tensor(f"{tag}ac0", [128, T_ALL], BF))
            acc1 = PE(nc.sbuf_tensor(f"{tag}ac1", [128, T_ALL], BF))
            q_bf = PE(nc.sbuf_tensor(f"{tag}qbf", [128, T_ALL], BF))
            k_bf = PE(nc.sbuf_tensor(f"{tag}kbf", [128, T_ALL], BF))
            v_bf = PE(nc.sbuf_tensor(f"{tag}vbf", [128, T_ALL], BF))
            o_ch = PE(nc.sbuf_tensor(f"{tag}och", [128, T_ALL], BF))
            S_f = PE(nc.sbuf_tensor(f"{tag}Sf", [128, 128], F32))
            S_b = PE(nc.sbuf_tensor(f"{tag}Sb", [128, 128], BF))
            NRW = T_ALL // 4
            onb1 = PE(nc.sbuf_tensor(f"{tag}onb1", [128, NRW], F32))
            onb2 = PE(nc.sbuf_tensor(f"{tag}onb2", [128, NRW], F32))
            on_t = PE(nc.sbuf_tensor(f"{tag}ont", [128, NRW], F32))
            nr1 = PE(nc.sbuf_tensor(f"{tag}nr1", [1, NRW], F32))
            nr2 = PE(nc.sbuf_tensor(f"{tag}nr2", [1, NRW], F32))
            nr3 = PE(nc.sbuf_tensor(f"{tag}nr3", [1, NRW], F32))
            pln = PS[6][0:1, :]
            bct = {}
            for nm in ("bG", "bD", "bU", "bA", "bQ", "bK"):
                bct[nm] = [PE(nc.sbuf_tensor(f"{tag}{nm}{i}", [128, 128], F32))
                           for i in range(2)]
            colsin = [PE(nc.sbuf_tensor(f"{tag}cli{i}", [8, 128], F32))
                      for i in range(2)]
            cols = [PE(nc.sbuf_tensor(f"{tag}cls{i}", [128, 8], F32))
                    for i in range(2)]
            eGf = PE(nc.sbuf_tensor(f"{tag}eGf", [128, 128], F32))
            eUf = PE(nc.sbuf_tensor(f"{tag}eUf", [128, 128], F32))
            eDf = PE(nc.sbuf_tensor(f"{tag}eDf", [128, 128], F32))
            t1f = PE(nc.sbuf_tensor(f"{tag}t1f", [128, 128], F32))
            gT = PE(nc.sbuf_tensor(f"{tag}gT", [128, 128], BF))
            gU = PE(nc.sbuf_tensor(f"{tag}gU", [128, 128], BF))
            aT = PE(nc.sbuf_tensor(f"{tag}aT", [128, 128], BF))
            f1T = PE(nc.sbuf_tensor(f"{tag}f1T", [128, 128], BF))
            bares = {l: (PE(nc.sbuf_tensor(f"{tag}b{l}T", [128, 128], BF)),
                         PE(nc.sbuf_tensor(f"{tag}b{l}U", [128, 128], BF)))
                     for l in (2, 4, 8, 16, 32)}
            fTs = {l: PE(nc.sbuf_tensor(f"{tag}f{l}T", [128, 128], BF))
                   for l in (2, 4, 8, 16, 32, 64)}
            knb = PE(nc.sbuf_tensor(f"{tag}knb", [128, 128], BF))
            qnb = PE(nc.sbuf_tensor(f"{tag}qnb", [128, 128], BF))
            ktm = PE(nc.sbuf_tensor(f"{tag}ktm", [128, 128], BF))
            qt_ = PE(nc.sbuf_tensor(f"{tag}qt", [128, 128], BF))
            am_ = PE(nc.sbuf_tensor(f"{tag}am", [128, 128], BF))
            xb = [PE(nc.sbuf_tensor(f"{tag}xb{i}", [128, 128], BF))
                  for i in range(2)]
            B0 = [PS[0], PS[1]]
            Bq1, Bq2, BO, B7 = PS[2], PS[3], PS[4], PS[5]
            B6 = PSB

            def conv(b, hh, ti, dst, wsl, bsl, scale=None):
                mz = p.op("vector", lambda e: e.memset(zp[:, 0:4], 0.0),
                          waits=wr("zp"), inc="SV")
                for qq in range(4):
                    if isa:
                        slot = ti * 2 + hh
                    else:
                        slot = [hh, 2, 3][ti]
                    p.dma("scalar", zp[:, 4 + qq * TOK:4 + (qq + 1) * TOK],
                          exd[4 * b + qq, slot],
                          waits=[("SV", mz), ("CC", p.val("CC"))]
                          if qq == 0 else (), inc="ZPQ")
                dz = p.val("ZPQ")
                c0 = p.op("vector", lambda e: e.tensor_scalar(
                    acc0[:], zp[:, 1:1 + T_ALL], wsl[0], None, ALU.mult),
                    waits=[("ZPQ", dz)] + wr("acc0"), inc="SV")
                c1 = p.op("vector", lambda e: e.scalar_tensor_tensor(
                    acc1[:], zp[:, 2:2 + T_ALL], wsl[1], acc0[:],
                    ALU.mult, ALU.add),
                    waits=[("SV", c0)] + wr("acc1"), inc="SV")
                c2 = p.op("vector", lambda e: e.scalar_tensor_tensor(
                    acc0[:], zp[:, 3:3 + T_ALL], wsl[2], acc1[:],
                    ALU.mult, ALU.add), waits=[("SV", c1)], inc="SV")
                c3 = p.op("vector", lambda e: e.scalar_tensor_tensor(
                    acc1[:], zp[:, 4:4 + T_ALL], wsl[3], acc0[:],
                    ALU.mult, ALU.add), waits=[("SV", c2)], inc="SV")
                rd("zp", "SV", c3)
                rd("acc0", "SV", c3)
                kw = dict(bias=bsl) if bsl is not None else {}
                sv = p.op("scalar", lambda e, kw=kw: e.activation(
                    dst[:], acc1[:], AF.Silu, **kw),
                    waits=[("SV", c3)] + wr(dst.name), inc="SS")
                rd("acc1", "SS", sv)
                if scale is not None:
                    v = p.op("vector", lambda e: e.tensor_scalar(
                        dst[:], dst[:], scale, None, ALU.mult),
                        waits=[("SS", sv)], inc="SV")
                    return ("SV", v)
                return ("SS", sv)

            def rnrow(c, src_bf, flvar, qscale):
                for qq in range(4):
                    qsl = slice(qq * NRW, (qq + 1) * NRW)
                    sq = p.op("scalar", lambda e, qsl=qsl: e.activation(
                        acc0[:, 0:NRW], src_bf[:, qsl], AF.Square),
                        waits=wr("acc0"), inc="SS")
                    ev = prevev = None
                    for t8 in range(NRW // 512):
                        w = [("SS", sq)] + (wr("pln") if t8 == 0 else
                                            [("SV", prevev)])
                        r = p.op("tensor", lambda e, t8=t8: e.matmul(
                            pln, s_onesb[:],
                            acc0[:, t8 * 512:(t8 + 1) * 512],
                            start=True, stop=True), waits=w, inc="ST")
                        ev = p.op("vector", lambda e, t8=t8: e.tensor_copy(
                            nr1[0:1, t8 * 512:(t8 + 1) * 512], pln),
                            waits=[("ST", r)] + (wr("nr1") if t8 == 0
                                                 else []), inc="SV")
                        prevev = ev
                    rd("pln", "SV", ev)
                    rd("acc0", "SV", ev)
                    l1 = p.op("scalar", lambda e: e.activation(
                        nr1[0:1, :], nr1[0:1, :], AF.Ln,
                        bias=s_cc[0:1, 3:4]), waits=[("SV", ev)], inc="SS")
                    if qscale:
                        l2 = p.op("scalar", lambda e: e.activation(
                            nr1[0:1, :], nr1[0:1, :], AF.Exp, scale=-0.5,
                            bias=s_cc[0:1, 2:3]), waits=[("SS", l1)],
                            inc="SS")
                    else:
                        l2 = p.op("scalar", lambda e: e.activation(
                            nr1[0:1, :], nr1[0:1, :], AF.Exp, scale=-0.5),
                            waits=[("SS", l1)], inc="SS")
                    d = p.dma("scalar", fl[flvar, c, qsl], nr1[0:1, :],
                              waits=[("SS", l2), ("FLQ", p.val("FLQ"))],
                              inc="FLQ")
                    rd("nr1", "FLQ", d)

            chains = [(0, 0), (0, 1), (1, 0), (1, 1)]
            for c, (b, hh) in enumerate(chains):
                if isa:
                    wsl_ = lambda ti: [s_cva[:, ti, hh, i:i + 1]
                                       for i in range(4)]
                    bsl_ = lambda ti: s_cba[:, ti, hh:hh + 1]
                    cq = conv(b, hh, 0, q_bf, wsl_(0), bsl_(0))
                    ck = conv(b, hh, 1, k_bf, wsl_(1), bsl_(1), scale=RT2)
                    cv_ = conv(b, hh, 2, v_bf, wsl_(2), bsl_(2))
                else:
                    cq = conv(b, hh, 0, q_bf,
                              [s_cvb[:, hh, i:i + 1] for i in range(4)], None)
                    if hh == 0:
                        ck = conv(b, hh, 1, k_bf,
                                  [s_cvb[:, 2, i:i + 1] for i in range(4)],
                                  None)
                        cv_ = conv(b, hh, 2, v_bf,
                                   [s_cvb[:, 3, i:i + 1] for i in range(4)],
                                   None)
                    rnrow(c, k_bf, 9, False)
                    rnrow(c, q_bf, 8, True)

                z1 = p.op("vector", lambda e: e.memset(S_f[:], 0.0),
                          waits=wr("S_f"), inc="SV")
                z2 = p.op("gpsimd", lambda e: e.memset(S_b[:], 0.0),
                          waits=[("SV", z1)] + wr("S_b"), inc="SG")
                sb_v = ("SG", z2)
                sf_v = ("SV", z1)
                rows_done = p.val("FLQ")

                for ci in range(NCHUNK):
                    par = ci % 2
                    sl = slice(ci * CH, (ci + 1) * CH)

                    def bcast(nm, var):
                        src = fl[var, c, sl].unsqueeze(0)
                        return p.dma("scalar", bct[nm][par][:],
                                     src.broadcast_to([128, CH]),
                                     waits=[("FLQ", rows_done)]
                                     + wr(f"{nm}{par}"), inc=f"Q{nm}{par}")

                    bcast("bG", 1)
                    bcast("bD", 6)
                    bcast("bU", 0)
                    bcast("bA", 7)
                    if not isa:
                        bcast("bQ", 8)
                        bcast("bK", 9)
                    dC = p.dma("scalar", colsin[par][0:6, :], fl[0:6, c, sl],
                               waits=[("FLQ", rows_done)] + wr(f"cli{par}"),
                               inc=f"QCL{par}")

                    tC = p.op("tensor", lambda e, par=par: e.transpose(
                        B7[:, 0:6], colsin[par][0:6, :], s_identf[0:6, 0:6]),
                        waits=[(f"QCL{par}", dC)] + wr("B7"), inc="ST")
                    rd(f"cli{par}", "ST", tC)
                    vC = p.op("vector", lambda e, par=par: e.tensor_copy(
                        cols[par][:, 0:6], B7[:, 0:6]),
                        waits=[("ST", tC)] + wr(f"cls{par}"), inc="SV")

                    if isa:
                        kn, qn = k_bf[:, sl], q_bf[:, sl]
                        kn_w, qn_w = [ck], [cq]
                    else:
                        vkn = p.op("vector", lambda e, par=par, sl=sl:
                                   e.tensor_tensor(knb[:], k_bf[:, sl],
                                                   bct["bK"][par][:],
                                                   ALU.mult),
                                   waits=[(f"QbK{par}", p.val(f"QbK{par}")),
                                          ck] + wr("knb"), inc="SV")
                        vqn = p.op("vector", lambda e, par=par, sl=sl:
                                   e.tensor_tensor(qnb[:], q_bf[:, sl],
                                                   bct["bQ"][par][:],
                                                   ALU.mult),
                                   waits=[(f"QbQ{par}", p.val(f"QbQ{par}")),
                                          cq] + wr("qnb"), inc="SV")
                        rd(f"bK{par}", "SV", vkn)
                        rd(f"bQ{par}", "SV", vqn)
                        kn, qn = knb[:], qnb[:]
                        kn_w, qn_w = [("SV", vkn)], [("SV", vqn)]

                    mK = p.op("tensor", lambda e, kn=kn, par=par: e.matmul(
                        B0[par][:, 0:128], kn, kn, start=True, stop=True),
                        waits=kn_w + wr(f"B0{par}"), inc="ST")
                    mQ = p.op("tensor", lambda e, kn=kn, qn=qn, par=par:
                              e.matmul(B0[par][:, 128:256], kn, qn,
                                       start=True, stop=True),
                              waits=qn_w, inc="ST")

                    exps = []
                    for (nm, colidx, mask, dst) in (
                            ("bG", 0, s_maskgt, eGf),
                            ("bU", 1, s_maskgu, eUf),
                            ("bD", 0, s_maskdt, eDf)):
                        vv = p.op("vector", lambda e, nm=nm, par=par,
                                  colidx=colidx, mask=mask, dst=dst:
                                  e.scalar_tensor_tensor(
                                      dst[:], bct[nm][par][:],
                                      cols[par][:, colidx:colidx + 1],
                                      mask[:], ALU.add, ALU.add),
                                  waits=[(f"Q{nm}{par}", p.val(f"Q{nm}{par}")),
                                         ("SV", vC)] + wr(dst.name),
                                  inc="SV")
                        rd(f"{nm}{par}", "SV", vv)
                        ss = p.op("scalar", lambda e, dst=dst: e.activation(
                            dst[:], dst[:], AF.Exp), waits=[("SV", vv)],
                            inc="SS")
                        exps.append(ss)
                    sE1, sE2, sE3 = exps

                    vGT = p.op("vector", lambda e, par=par: e.tensor_tensor(
                        gT[:], B0[par][:, 0:128], eGf[:], ALU.mult),
                        waits=[("SS", sE1), ("ST", mK)] + wr("gT"), inc="SV")
                    rd(eGf.name, "SV", vGT)
                    vGU = p.op("vector", lambda e, par=par: e.tensor_tensor(
                        gU[:], B0[par][:, 0:128], eUf[:], ALU.mult),
                        waits=[("SS", sE2)] + wr("gU"), inc="SV")
                    rd(eUf.name, "SV", vGU)
                    vAT = p.op("vector", lambda e, par=par: e.tensor_tensor(
                        aT[:], B0[par][:, 128:256], eDf[:], ALU.mult),
                        waits=[("SS", sE3), ("ST", mQ)] + wr("aT"), inc="SV")
                    rd(eDf.name, "SV", vAT)
                    for vv in (vGT, vGU, vAT):
                        rd(f"B0{par}", "SV", vv)

                    gf1 = p.op("gpsimd", lambda e: e.tensor_tensor(
                        f1T[:], s_identb[:], gT[:], ALU.subtract),
                        waits=[("SV", vGT)] + wr("f1T"), inc="SG")

                    prevT = ("gT", gT, ("SV", vGT))
                    prevU = ("gU", gU, ("SV", vGU))
                    fvals = {1: ("SG", gf1)}
                    for l in (2, 4, 8, 16, 32):
                        bT, bU = bares[l]
                        pTn, pT, pTv = prevT
                        pUn, pU, pUv = prevU
                        m1 = p.op("tensor", lambda e, pT=pT, pU=pU: e.matmul(
                            Bq1[:, 0:128], pU[:], pT[:], start=True,
                            stop=True), waits=[pTv, pUv] + wr("Bq1"),
                            inc="ST")
                        m2 = p.op("tensor", lambda e, pT=pT, pU=pU: e.matmul(
                            Bq2[:, 0:128], pT[:], pU[:], start=True,
                            stop=True), waits=wr("Bq2"), inc="ST")
                        rd(pTn, "ST", m2)
                        rd(pUn, "ST", m2)
                        eT = p.op("vector", lambda e, bT=bT: e.tensor_copy(
                            bT[:], Bq1[:, 0:128]),
                            waits=[("ST", m1)] + wr(bT.name), inc="SV")
                        rd("Bq1", "SV", eT)
                        eU = p.op("vector", lambda e, bU=bU: e.tensor_copy(
                            bU[:], Bq2[:, 0:128]),
                            waits=[("ST", m2)] + wr(bU.name), inc="SV")
                        rd("Bq2", "SV", eU)
                        fv = p.op("gpsimd", lambda e, bT=bT, l=l:
                                  e.tensor_tensor(fTs[l][:], bT[:],
                                                  s_identb[:], ALU.add),
                                  waits=[("SV", eT)] + wr(fTs[l].name),
                                  inc="SG")
                        fvals[l] = ("SG", fv)
                        prevT = (bT.name, bT, ("SV", eT))
                        prevU = (bU.name, bU, ("SV", eU))
                    m64 = p.op("tensor", lambda e: e.matmul(
                        Bq1[:, 0:128], bares[32][1][:], bares[32][0][:],
                        start=True, stop=True),
                        waits=[prevT[2], prevU[2]] + wr("Bq1"), inc="ST")
                    rd(bares[32][0].name, "ST", m64)
                    rd(bares[32][1].name, "ST", m64)
                    e64 = p.op("vector", lambda e: e.tensor_tensor(
                        fTs[64][:], Bq1[:, 0:128], s_identb[:], ALU.add),
                        waits=[("ST", m64)] + wr(fTs[64].name),
                        inc="SV")
                    rd("Bq1", "SV", e64)
                    fvals[64] = ("SV", e64)

                    tV = p.op("tensor", lambda e, sl=sl: e.transpose(
                        B6[:, 0:128], v_bf[:, sl], s_identb[:]),
                        waits=[cv_] + wr("B6"), inc="ST")
                    tK = p.op("tensor", lambda e, kn=kn: e.transpose(
                        B6[:, 128:256], kn, s_identb[:]),
                        waits=kn_w, inc="ST")

                    mY = p.op("tensor", lambda e, kn=kn: e.matmul(
                        B7[:, 128:256], kn, S_b[:], start=True, stop=True),
                        waits=[sb_v, ("SV", vC)] + wr("B7y"), inc="ST")
                    vt1 = p.op("vector", lambda e, par=par: e.tensor_scalar(
                        t1f[:], B7[:, 128:256], cols[par][:, 2:3], None,
                        ALU.mult), waits=[("ST", mY)] + wr("t1f"), inc="SV")
                    rd("B7y", "SV", vt1)
                    vX0 = p.op("vector", lambda e, par=par:
                               e.scalar_tensor_tensor(
                                   xb[0][:], B6[:, 0:128],
                                   cols[par][:, 3:4], t1f[:],
                                   ALU.mult, ALU.subtract),
                               waits=[("ST", tK), ("SV", vt1)] + wr("xb0"),
                               inc="SV")
                    rd("B6", "SV", vX0)
                    rd("t1f", "SV", vX0)
                    vKT = p.op("vector", lambda e: e.tensor_copy(
                        ktm[:], B6[:, 128:256]),
                        waits=[("ST", tK)] + wr("ktm"), inc="SV")
                    rd("B6", "SV", vKT)

                    xcur, xv = 0, ("SV", vX0)
                    for j, l in enumerate((64, 32, 16, 8, 4, 2, 1)):
                        bank, bn = (Bq1, "Bq1") if j % 2 == 0 else (Bq2, "Bq2")
                        ftile = f1T if l == 1 else fTs[l]
                        fnm = "f1T" if l == 1 else fTs[l].name
                        ma = p.op("tensor", lambda e, ftile=ftile, bank=bank,
                                  xcur=xcur: e.matmul(
                                      bank[:, 0:128], ftile[:], xb[xcur][:],
                                      start=True, stop=True),
                                  waits=[fvals[l], xv] + wr(bn), inc="ST")
                        rd(f1T.name if l == 1 else fTs[l].name,
                           "ST", ma)
                        rd(f"xb{xcur}", "ST", ma)
                        nxt = 1 - xcur
                        ev = p.op("vector", lambda e, bank=bank, nxt=nxt:
                                  e.tensor_copy(xb[nxt][:], bank[:, 0:128]),
                                  waits=[("ST", ma)] + wr(f"xb{nxt}"),
                                  inc="SV")
                        rd(bn, "SV", ev)
                        xcur, xv = nxt, ("SV", ev)

                    vqt = p.op("vector", lambda e, par=par, qn=qn:
                               e.tensor_tensor(qt_[:], qn, bct["bA"][par][:],
                                               ALU.mult),
                               waits=[(f"QbA{par}", p.val(f"QbA{par}"))]
                               + qn_w + wr("qt"), inc="SV")
                    rd(f"bA{par}", "SV", vqt)

                    mO1 = p.op("tensor", lambda e, xcur=xcur: e.matmul(
                        BO[:, 0:128], xb[xcur][:], aT[:], start=True,
                        stop=False),
                        waits=[xv, ("SV", vAT), ("SV", vqt)] + wr("BO"),
                        inc="ST")
                    mO2 = p.op("tensor", lambda e: e.matmul(
                        BO[:, 0:128], S_b[:], qt_[:], start=False, stop=True),
                        waits=[sb_v], inc="ST")
                    rd("aT", "ST", mO1)
                    rd("qt", "ST", mO2)
                    vO = p.op("vector", lambda e, sl=sl: e.tensor_copy(
                        o_ch[:, sl], BO[:, 0:128]),
                        waits=[("ST", mO2)] + (wr("o_ch") if ci == 0 else []),
                        inc="SV")
                    rd("BO", "SV", vO)

                    vAM = p.op("vector", lambda e, par=par, xcur=xcur:
                               e.tensor_scalar(am_[:], xb[xcur][:],
                                               cols[par][:, 4:5], None,
                                               ALU.mult),
                               waits=[xv] + wr("am"), inc="SV")
                    rd(f"xb{xcur}", "SV", vAM)
                    mS = p.op("tensor", lambda e: e.matmul(
                        B7[:, 256:384], ktm[:], am_[:], start=True,
                        stop=True),
                        waits=[("SV", vKT), ("SV", vAM), ("SV", vt1)]
                        + wr("B7s"), inc="ST")
                    rd("ktm", "ST", mS)
                    rd("am", "ST", mS)
                    vSf = p.op("vector", lambda e, par=par:
                               e.scalar_tensor_tensor(
                                   S_f[:], S_f[:], cols[par][:, 5:6],
                                   B7[:, 256:384], ALU.mult, ALU.add),
                               waits=[("ST", mS), sf_v, ("ST", mO2)],
                               inc="SV")
                    rd("B7s", "SV", vSf)
                    rd(f"cls{par}", "SV", vSf)
                    gSb = p.op("gpsimd", lambda e: e.tensor_copy(
                        S_b[:], S_f[:]),
                        waits=[("SV", vSf), ("ST", mO2), ("ST", mY)],
                        inc="SG")
                    sf_v = ("SV", vSf)
                    sb_v = ("SG", gSb)
                    rd("B7", "SV", vC)
                    rd("B7", "SV", vt1)
                    rd("B7", "SV", vSf)

                for qq in range(4):
                    slq = slice(qq * NRW, (qq + 1) * NRW)
                    sq = p.op("scalar", lambda e, slq=slq: e.activation(
                        zp[:, 0:NRW], o_ch[:, slq], AF.Square),
                        waits=wr("zp") + ([("SV", p.val("SV"))]
                                          if qq == 0 else []), inc="SS")
                    evm = evs = None
                    prevr = None
                    for t8 in range(NRW // 512):
                        sl8 = slice(qq * NRW + t8 * 512,
                                    qq * NRW + (t8 + 1) * 512)
                        if isa:
                            w = ([("SV", p.val("SV"))] if (qq == 0 and
                                                           t8 == 0) else [])
                            w += wr("pln") if (qq == 0 and t8 == 0) else \
                                ([("SV", prevr)] if prevr else wr("pln"))
                            r1 = p.op("tensor", lambda e, sl8=sl8: e.matmul(
                                pln, s_onesb[:], o_ch[:, sl8], start=True,
                                stop=True), waits=w, inc="ST")
                            evm = p.op("vector", lambda e, t8=t8:
                                       e.tensor_scalar(
                                           nr2[0:1, t8 * 512:(t8 + 1) * 512],
                                           pln, 1.0 / 128, None, ALU.mult),
                                       waits=[("ST", r1)]
                                       + (wr("nr2") if t8 == 0 else []),
                                       inc="SV")
                            prevr = evm
                        w = [("SS", sq)] + ([("SV", prevr)] if prevr
                                            else (wr("pln") if t8 == 0
                                                  else []))
                        r2 = p.op("tensor", lambda e, t8=t8: e.matmul(
                            pln, s_onesb[:], zp[:, t8 * 512:(t8 + 1) * 512],
                            start=True, stop=True), waits=w, inc="ST")
                        evs = p.op("vector", lambda e, t8=t8: e.tensor_scalar(
                            nr3[0:1, t8 * 512:(t8 + 1) * 512], pln,
                            1.0 / 128, None, ALU.mult),
                            waits=[("ST", r2)] + (wr("nr3") if t8 == 0
                                                  else []), inc="SV")
                        prevr = evs
                    rd("pln", "SV", evs)
                    rd("zp", "SV", evs)
                    if isa:
                        vm2 = p.op("vector", lambda e: e.tensor_tensor(
                            nr1[0:1, :], nr2[0:1, :], nr2[0:1, :], ALU.mult),
                            waits=[("SV", evm)] + wr("nr1"), inc="SV")
                        vvar = p.op("vector", lambda e: e.tensor_tensor(
                            nr3[0:1, :], nr3[0:1, :], nr1[0:1, :],
                            ALU.subtract),
                            waits=[("SV", vm2), ("SV", evs)], inc="SV")
                    else:
                        vvar = evs
                    sl1 = p.op("scalar", lambda e: e.activation(
                        nr3[0:1, :], nr3[0:1, :], AF.Ln,
                        bias=s_cc[0:1, 0:1]), waits=[("SV", vvar)], inc="SS")
                    sl2 = p.op("scalar", lambda e: e.activation(
                        nr3[0:1, :], nr3[0:1, :], AF.Exp, scale=-0.5),
                        waits=[("SS", sl1)], inc="SS")
                    if isa:
                        vmr = p.op("vector", lambda e: e.tensor_tensor(
                            nr1[0:1, :], nr2[0:1, :], nr3[0:1, :], ALU.mult),
                            waits=[("SS", sl2)], inc="SV")
                        rd("nr2", "SV", vmr)
                        dn1 = p.dma("scalar", onflat[0:1, slq], nr3[0:1, :],
                                    waits=[("SS", sl2)], inc="ONQ")
                        dn2 = p.dma("scalar", onflat[1:2, slq], nr1[0:1, :],
                                    waits=[("SV", vmr), ("ONQ", dn1)],
                                    inc="ONQ")
                        rd("nr1", "ONQ", p.val("ONQ"))
                        rd("nr3", "ONQ", p.val("ONQ"))
                    else:
                        dn1 = p.dma("scalar", onflat[0:1, slq], nr3[0:1, :],
                                    waits=[("SS", sl2)], inc="ONQ")
                        rd("nr3", "ONQ", p.val("ONQ"))
                    b1 = p.dma("scalar", onb1[:],
                               onflat[0:1, slq].broadcast_to([128, NRW]),
                               waits=[("ONQ", p.val("ONQ"))] + wr("onb1"),
                               inc="OB1")
                    vt = p.op("vector", lambda e, slq=slq: e.tensor_tensor(
                        on_t[:], o_ch[:, slq], onb1[:], ALU.mult),
                        waits=[("OB1", b1)] + wr("on_t"), inc="SV")
                    rd("onb1", "SV", vt)
                    if isa:
                        b2 = p.dma("scalar", onb2[:],
                                   onflat[1:2, slq].broadcast_to([128, NRW]),
                                   waits=wr("onb2"), inc="OB2")
                        g1 = p.op("gpsimd", lambda e: e.tensor_tensor(
                            on_t[:], on_t[:], onb2[:], ALU.subtract),
                            waits=[("SV", vt), ("OB2", b2)], inc="SG")
                        rd("onb2", "SG", g1)
                        vo = p.op("vector", lambda e, slq=slq: e.tensor_scalar(
                            acc1[:, slq], on_t[:], s_lnwb[:, 0:1],
                            s_lnwb[:, 1:2], ALU.mult, ALU.add),
                            waits=[("SG", g1)] + (wr("acc1") if qq == 0
                                                  else []), inc="SV")
                    else:
                        vo = p.op("vector", lambda e, slq=slq: e.tensor_scalar(
                            acc1[:, slq], on_t[:], s_onwb[:, 0:1], None,
                            ALU.mult),
                            waits=[("SV", vt)] + (wr("acc1") if qq == 0
                                                  else []), inc="SV")
                    rd("on_t", "SV", vo)
                    dS = p.dma("scalar", oxs[4 * b + qq, hh], acc1[:, slq],
                               waits=[("SV", vo), ("OXQ", p.val("OXQ"))],
                               inc="OXQ")
                    rd("acc1", "OXQ", dS)
                rd("o_ch", "OXQ", p.val("OXQ"))
                pst = p.val("ST")
                for nm in (q_bf, k_bf, v_bf):
                    rd(nm.name, "ST", pst)
                rd("S_f", "SV", p.val("SV"))
                rd("S_b", "ST", pst)
            p.barrier()

    def ffn(which):
        tag = "F" + which
        cache = which == "2"
        with contextlib.ExitStack() as ph:
            PE = ph.enter_context
            xln = PE(nc.sbuf_tensor(f"{tag}xln", [128, KC, TOK], BF))
            with contextlib.ExitStack() as lh:
                layernorm(tag + "l", lh.enter_context, xln)
                p.barrier()
            h_bf = PE(nc.sbuf_tensor(f"{tag}h", [128, 32, TOK], BF))
            wld = [PE(nc.sbuf_tensor(f"{tag}wld{i}", [128, KC * 128], F32))
                   for i in range(2)]
            wbf = [PE(nc.sbuf_tensor(f"{tag}wbf{i}", [128, KC * 128], BF))
                   for i in range(4)]
            sg = [PE(nc.sbuf_tensor(f"{tag}sg{i}", [128, 512], BF))
                  for i in range(TH)]
            pg = [PS[i] for i in range(TH)]
            pu = [PS[2 + i] for i in range(TH)]

            def stream_col(bi, src_ap, cache_ap):
                if cache:
                    ld = p.dma("sync",
                               wbf[bi][:].rearrange("p (k f) -> p k f", k=KC),
                               src_ap.rearrange("k p f -> p k f"),
                               waits=wr(f"wbf{tag}{bi}"), inc=f"{tag}L{bi}")
                    return [(f"{tag}L{bi}", ld)]
                wi = bi % 2
                ld = p.dma("sync",
                           wld[wi][:].rearrange("p (k f) -> p k f", k=KC),
                           src_ap.rearrange("k p f -> p k f"),
                           waits=wr(f"wld{tag}{wi}"), inc=f"{tag}L{bi}")
                cv = p.op("vector", lambda e, bi=bi, wi=wi: e.tensor_copy(
                    wbf[bi][:], wld[wi][:]),
                    waits=[(f"{tag}L{bi}", ld)] + wr(f"wbf{tag}{bi}"),
                    inc="SV")
                rd(f"wld{tag}{wi}", "SV", cv)
                if cache_ap is not None:
                    dc = p.dma("scalar", cache_ap.rearrange("k p f -> p k f"),
                               wbf[bi][:].rearrange("p (k f) -> p k f", k=KC),
                               waits=[("SV", cv)], inc=f"WC{tag}{bi}")
                    rd(f"wbf{tag}{bi}", f"WC{tag}{bi}", dc)
                return [("SV", cv)]

            for w in range(2):
                for n_ in range(32):
                    gidx, uidx = 32 * w + n_, 64 + 32 * w + n_
                    bg, bu = (n_ % 2) * 2, (n_ % 2) * 2 + 1
                    wsrc = wgu_bf if cache else wgu
                    gw = stream_col(bg, wsrc[gidx],
                                    None if cache else wgu_bf[gidx])
                    uw = stream_col(bu, wsrc[uidx],
                                    None if cache else wgu_bf[uidx])
                    for half, bi, pp_, ww in ((0, bg, pg, gw), (1, bu, pu, uw)):
                        for th in range(TH):
                            r = None
                            for k in range(KC):
                                r = p.op("tensor",
                                         lambda e, bi=bi, k=k, th=th, pp_=pp_:
                                         e.matmul(
                                             pp_[th][:],
                                             wbf[bi][:, k * 128:(k + 1) * 128],
                                             xln[:, k, th * 512:(th + 1) * 512],
                                             start=(k == 0),
                                             stop=(k == KC - 1)),
                                         waits=(ww + wr(f"p{'g' if half == 0 else 'u'}{th}"))
                                         if k == 0 else (),
                                         inc=f"{tag}M" if k == KC - 1
                                         else None)
                            if half == 0:
                                gmm = r
                                ss = p.op("scalar", lambda e, th=th:
                                          e.activation(sg[th][:], pg[th][:],
                                                       AF.Silu),
                                          waits=[(f"{tag}M", r)]
                                          + wr(f"sg{th}"), inc="SS")
                                rd(f"pg{th}", "SS", ss)
                                sgv = ss
                            else:
                                hv = p.op("vector",
                                          lambda e, th=th, n_=n_:
                                          e.tensor_tensor(
                                              h_bf[:, n_,
                                                   th * 512:(th + 1) * 512],
                                              pu[th][:], sg[th][:], ALU.mult),
                                          waits=[(f"{tag}M", r),
                                                 ("SS", sgv)]
                                          + (wr(f"h{n_}") if w == 1 else []),
                                          inc=f"{tag}E")
                                rd(f"pu{th}", f"{tag}E", hv)
                                rd(f"sg{th}", f"{tag}E", hv)
                    mv = p.val(f"{tag}M")
                    rd(f"wbf{tag}{bg}", f"{tag}M", mv)
                    rd(f"wbf{tag}{bu}", f"{tag}M", mv)
                hv_all = p.val(f"{tag}E")
                for o in range(KC):
                    bds = [(o % 2) * 2, (o % 2) * 2 + 1]
                    wsrcd = wdn_bf if cache else wdn
                    cws = []
                    for ks in range(2):
                        src = wsrcd[o, 32 * w + 16 * ks:32 * w + 16 * (ks + 1)]
                        cap = (None if cache else
                               wdn_bf[o, 32 * w + 16 * ks:
                                      32 * w + 16 * (ks + 1)])
                        cws.append(stream_col(bds[ks], src, cap))
                    for th in range(TH):
                        r = None
                        for ks in range(2):
                            for kk in range(KC):
                                kglob = 16 * ks + kk
                                fw = []
                                if ks == 0 and kk == 0:
                                    fw = cws[0] + [(f"{tag}E", hv_all)] \
                                        + wr(f"pg{th}")
                                elif ks == 1 and kk == 0:
                                    fw = cws[1]
                                r = p.op("tensor",
                                         lambda e, bi=bds[ks], kk=kk, th=th,
                                         kglob=kglob, ks=ks: e.matmul(
                                             pg[th][:],
                                             wbf[bi][:, kk * 128:(kk + 1) * 128],
                                             h_bf[:, kglob,
                                                  th * 512:(th + 1) * 512],
                                             start=(ks == 0 and kk == 0),
                                             stop=(ks == 1 and kk == KC - 1),
                                             skip_group_check=True),
                                         waits=fw,
                                         inc=f"{tag}M"
                                         if ks == 1 and kk == KC - 1
                                         else None)
                        de = p.op("vector", lambda e, o=o, th=th:
                                  e.tensor_tensor(
                                      x_fm[:, o, th * 512:(th + 1) * 512],
                                      pg[th][:],
                                      x_fm[:, o, th * 512:(th + 1) * 512],
                                      ALU.add),
                                  waits=[(f"{tag}M", r)], inc=f"{tag}DE")
                        rd(f"pg{th}", f"{tag}DE", de)
                    mv = p.val(f"{tag}M")
                    rd(f"wbf{tag}{bds[0]}", f"{tag}M", mv)
                    rd(f"wbf{tag}{bds[1]}", f"{tag}M", mv)
                if w == 0:
                    for n_ in range(32):
                        rd(f"h{n_}", f"{tag}M", p.val(f"{tag}M"))
            p.barrier()

    if "attn_a" in stages:
        attention("attn_a")
    if "ffn1" in stages:
        ffn("1")
    if "attn_b" in stages:
        attention("attn_b")
    if "ffn2" in stages:
        ffn("2")

    p.barrier()
    p.dma("sync", y_out, x_fm[:], inc="QS")
    p.barrier()

    with nc.Block() as block, allow_ici_all_to_all():
        p.emit(nc, block, gstk)
    gstk.close()
    return nc


def _pretile(W, fw=128):
    din_, dout = W.shape
    return np.ascontiguousarray(
        W.reshape(din_ // 128, 128, dout // fw, fw).transpose(2, 0, 1, 3))


def _prep(inputs, T_ALL=4096):
    TOK = T_ALL // 4
    f = np.float32
    x = np.asarray(inputs["L0_attn_a_x"], f)

    wqkvg_a = np.concatenate([_pretile(np.asarray(inputs[k], f)) for k in
                              ("L0_attn_a_Wq", "L0_attn_a_Wk", "L0_attn_a_Wv",
                               "L0_attn_a_Wg")], axis=0)
    wab_a = _pretile(np.concatenate([np.asarray(inputs["L0_attn_a_Wa"], f),
                                     np.asarray(inputs["L0_attn_a_Wb"], f)],
                                    axis=1), fw=32)
    wo_a = _pretile(np.asarray(inputs["L0_attn_a_Wo"], f))
    wqkvg_b = np.concatenate(
        [_pretile(np.ascontiguousarray(np.asarray(inputs["L0_attn_b_qkv_w"],
                                                  f).T)),
         _pretile(np.ascontiguousarray(np.asarray(inputs["L0_attn_b_g_w"],
                                                  f).T))], axis=0)
    wab_b = _pretile(np.ascontiguousarray(np.concatenate(
        [np.asarray(inputs["L0_attn_b_gk_w"], f),
         np.asarray(inputs["L0_attn_b_b_w"], f)], axis=0).T), fw=32)
    wo_b = _pretile(np.ascontiguousarray(np.asarray(inputs["L0_attn_b_o_w"],
                                                    f).T))
    wgu = _pretile(np.ascontiguousarray(np.asarray(inputs["L0_ffn_w_gate_up"],
                                                   f).T))
    wdn = _pretile(np.ascontiguousarray(np.asarray(inputs["L0_ffn_w_down"],
                                                   f).T))

    abb_a = np.ascontiguousarray(
        np.concatenate([np.asarray(inputs["L0_attn_a_ba"], f),
                        np.asarray(inputs["L0_attn_a_bb"], f)])[:, None])
    lnwb_a = np.ascontiguousarray(
        np.stack([np.asarray(inputs["L0_attn_a_ln_w"], f),
                  np.asarray(inputs["L0_attn_a_ln_b"], f)], axis=1))
    onw_b = np.ascontiguousarray(
        np.asarray(inputs["L0_attn_b_out_norm_w"], f)[:, None])

    identb = np.eye(128, dtype=BF16)
    identf = np.eye(128, dtype=np.float32)
    ones_b = np.ones((128, 1), BF16)
    ii = np.arange(128)
    maskgt = np.where(ii[:, None] < ii[None, :], 0.0, -1e4).astype(f)
    maskgu = np.where(ii[:, None] > ii[None, :], 0.0, -1e4).astype(f)
    maskdt = np.where(ii[:, None] <= ii[None, :], 0.0, -1e4).astype(f)

    cw = [np.asarray(inputs[k], f).reshape(2048, 4) for k in
          ("L0_attn_a_conv_q_w", "L0_attn_a_conv_k_w", "L0_attn_a_conv_v_w")]
    cb = [np.asarray(inputs[k], f) for k in
          ("L0_attn_a_conv_q_b", "L0_attn_a_conv_k_b", "L0_attn_a_conv_v_b")]
    cwb = np.asarray(inputs["L0_attn_b_conv_w"], f)

    shared = dict(wqkvg_a=wqkvg_a, wab_a=wab_a, wo_a=wo_a, wqkvg_b=wqkvg_b,
                  wab_b=wab_b, wo_b=wo_b, wgu=wgu, wdn=wdn, abb_a=abb_a,
                  lnwb_a=lnwb_a, onw_b=onw_b, identb=identb, identf=identf,
                  ones_b=ones_b, maskgt=maskgt, maskgu=maskgu, maskdt=maskdt)

    in_maps = []
    for core in range(8):
        b, qq = core // 4, core % 4
        xs = x[b, qq * TOK:(qq + 1) * TOK, :]
        x_fm = np.ascontiguousarray(
            xs.T.reshape(KC, 128, TOK).transpose(1, 0, 2))
        h0, h1 = 2 * core, 2 * core + 1
        cva_ = np.zeros((128, 3, 2, 4), f)
        cba_ = np.zeros((128, 3, 2), f)
        for ti in range(3):
            for hh, h in enumerate((h0, h1)):
                cva_[:, ti, hh, :] = cw[ti][h * 128:(h + 1) * 128, :]
                cba_[:, ti, hh] = cb[ti][h * 128:(h + 1) * 128]
        cvb_ = np.zeros((128, 4, 4), f)
        kv = core // 2
        cvb_[:, 0, :] = cwb[h0 * 128:(h0 + 1) * 128, :]
        cvb_[:, 1, :] = cwb[h1 * 128:(h1 + 1) * 128, :]
        cvb_[:, 2, :] = cwb[2048 + kv * 128:2048 + (kv + 1) * 128, :]
        cvb_[:, 3, :] = cwb[2560 + kv * 128:2560 + (kv + 1) * 128, :]
        in_maps.append(dict(x=x_fm, cva=cva_, cba=cba_, cvb=cvb_, **shared))
    return in_maps


def kernel(**inputs):
    T_ALL = 4096
    in_maps = _prep(inputs, T_ALL)
    nc = build(T_ALL)
    res = run_bass_kernel_spmd(nc, in_maps, core_ids=list(range(8)))
    TOK = T_ALL // 4
    out = np.zeros((2, T_ALL, D), np.float32)
    for core in range(8):
        b, qq = core // 4, core % 4
        y = np.asarray(res.results[core]["y"])
        out[b, qq * TOK:(qq + 1) * TOK, :] = \
            y.transpose(2, 1, 0).reshape(TOK, D)
    return out
